# version 01
# baseline (speedup 1.0000x reference)
import math
import numpy as np
import ml_dtypes
from contextlib import ExitStack

import concourse.bass as bass
import concourse.mybir as mybir
from concourse.bass_utils import run_bass_kernel_spmd

F32 = mybir.dt.float32
BF16 = mybir.dt.bfloat16
AF = mybir.ActivationFunctionType
ALU = mybir.AluOpType
AX = mybir.AxisListType

NCORE = 8
T = 2048
D = 1024
NEG = -1.0e9
BF = ml_dtypes.bfloat16


def _np(x):
    return np.ascontiguousarray(np.asarray(x, dtype=np.float32))


def tile_lhsT(w):
    K, M = w.shape
    Kt, Mt = K // 128, M // 128
    return _np(w.reshape(Kt, 128, Mt, 128).transpose(1, 0, 2, 3))


def tile_rhs(w, nw):
    K, N = w.shape
    Kt, Nt = K // 128, N // nw
    return _np(w.reshape(Kt, 128, Nt, nw).transpose(1, 0, 2, 3))


def col128(b):
    b = _np(b)
    n = b.shape[0] // 128
    return _np(b.reshape(n, 128).T)


def make_consts():
    c = {}
    c["ones_col"] = np.ones((128, 1), np.float32)
    c["ones_row"] = np.ones((1, 128), np.float32)
    c["ident"] = np.eye(128, dtype=np.float32)
    c["iota8"] = np.broadcast_to(np.arange(8, dtype=np.float32), (128, 8)).copy()
    tri = np.zeros((128, 128), np.float32)
    for tp in range(128):
        tri[tp, tp + 1:] = 1.0
    c["tri_excl"] = tri
    i = np.arange(128)[:, None]
    j = np.arange(640)[None, :]
    c["mask0"] = np.where((j >= i) & (j <= i + 512), 0.0, NEG).astype(BF)
    c["neg_tile"] = np.full((128, 512), NEG, np.float32)
    n_heads = 16
    start = 2 ** (-2 ** (-(math.log2(n_heads) - 3)))
    slopes = np.array([start * start ** h for h in range(n_heads)], np.float32)
    c["alibi_flat"] = _np(8.0 * slopes[:, None] * (np.arange(640, dtype=np.float32)[None, :] - 512.0))
    em = np.zeros((128, 4 * 512), np.float32)
    for qb in range(4):
        jmax = 512 - qb * 128
        em[:, qb * 512: qb * 512 + jmax] = NEG
    c["edge_mask_even"] = em.astype(BF)
    c["edge_mask_odd"] = np.zeros((128, 4 * 512), BF)
    c["iota512"] = np.broadcast_to(np.arange(512, dtype=np.float32), (128, 512)).copy()
    return c


def prep_weights(inputs):
    P = dict(make_consts())
    W1 = _np(inputs["L0_ffn_W1"])
    w1t = W1.transpose(0, 2, 1).reshape(8, 8, 128, 2, 16, 128)
    P["l0_w1t"] = _np(w1t.transpose(0, 2, 4, 3, 1, 5))
    W2 = _np(inputs["L0_ffn_W2"])
    w2t = W2.transpose(0, 2, 1).reshape(8, 16, 128, 8, 128)
    P["l0_w2t"] = _np(w2t.transpose(0, 2, 1, 3, 4))
    P["l0_wr"] = tile_rhs(_np(inputs["L0_ffn_Wr"]).T, 8)
    P["mb_inprojT"] = tile_lhsT(_np(inputs["L0_attn_in_proj_w"]).T)
    P["mb_inproj_b"] = col128(inputs["L0_attn_in_proj_b"])
    cw = _np(np.asarray(inputs["L0_attn_conv_w"])[:, 0, :])
    P["mb_convw_col"] = _np(cw.reshape(8, 128, 4).transpose(1, 0, 2).reshape(128, 32))
    P["mb_convb_col"] = col128(inputs["L0_attn_conv_b"])
    P["mb_bcT"] = tile_rhs(_np(inputs["L0_attn_bc_w"]).T, 32)
    P["mb_bc_b"] = _np(np.asarray(inputs["L0_attn_bc_b"]).reshape(1, 32))
    P["mb_dtT"] = tile_lhsT(_np(inputs["L0_attn_dt_w"]).T)
    P["mb_dt_b"] = col128(inputs["L0_attn_dt_b"])
    P["mb_Alog_col"] = _np(_np(inputs["L0_attn_A_log"]).reshape(8, 128, 16)
                           .transpose(1, 0, 2).reshape(128, 128))
    P["mb_D_col"] = col128(inputs["L0_attn_D"])
    P["mb_outT"] = tile_lhsT(_np(inputs["L0_attn_out_w"]).T)
    P["mb_out_b"] = col128(inputs["L0_attn_out_b"])
    P["mb_lng"] = col128(inputs["L0_attn_ln_g"])
    P["mb_lnb"] = col128(inputs["L0_attn_ln_b"])
    P["at_wqT"] = tile_lhsT(_np(inputs["L1_attn_wq"]).T)
    P["at_wkT"] = tile_lhsT(_np(inputs["L1_attn_wk"]).T)
    P["at_wvT"] = tile_rhs(_np(inputs["L1_attn_wv"]).T, 512)
    P["at_woT"] = tile_lhsT(_np(inputs["L1_attn_wo"]).T)
    P["at_qnw"] = col128(inputs["L1_attn_q_norm_w"])
    P["at_knw"] = col128(inputs["L1_attn_k_norm_w"])
    P["l1_wr"] = tile_rhs(_np(inputs["L1_ffn_router_w"]), 8)
    H1, HP = 5464, 5504
    w0 = _np(inputs["L1_ffn_w0_kernel"]); w1_ = _np(inputs["L1_ffn_w1_kernel"])
    wo = _np(inputs["L1_ffn_wo_kernel"])
    w0p = np.zeros((8, 1024, HP), np.float32); w0p[:, :, :H1] = w0
    w1p = np.zeros((8, 1024, HP), np.float32); w1p[:, :, :H1] = w1_
    wop = np.zeros((8, HP, 1024), np.float32); wop[:, :H1, :] = wo

    def l1w01(w):
        return w.reshape(8, 8, 128, 43, 128).transpose(0, 3, 1, 2, 4)
    _w01 = np.stack([l1w01(w0p), l1w01(w1p)], axis=2)
    P["l1_w01t"] = _np(_w01.transpose(0, 1, 4, 2, 3, 5))
    P["l1_wot"] = _np(wop.reshape(8, 43, 128, 2, 512))
    return P



ENGINES = ("sync", "gpsimd", "tensor", "vector", "scalar")


class _Tok:
    __slots__ = ("rec",)
    def __init__(self, rec):
        self.rec = rec
    def inc(self, sem, k=1):
        sem.n += k
        self.rec.append(("inc", sem, k))
        return self


_NOCHAIN = {"dma_start", "dma_start_transpose", "nop", "wait_ge"}


class _Rec:
    def __init__(self, eng, auto=None):
        self._eng = eng
        self._ops = []
        self._auto = auto

    def __getattr__(self, op):
        if op.startswith("_"):
            raise AttributeError(op)
        def f(*args, **kw):
            if self._auto is not None and op not in _NOCHAIN:
                self.wait(self._auto)
            rec = [("op", op, args, kw)]
            self._ops.append(rec)
            tok = _Tok(rec)
            if self._auto is not None and op not in _NOCHAIN:
                tok.inc(self._auto)
            return tok
        return f

    def wait(self, sem, val=None):
        v = sem.n if val is None else val
        self._ops.append([("wait", sem, v)])

    def replay(self, b):
        for rec in self._ops:
            ins = None
            for item in rec:
                if item[0] == "op":
                    ins = getattr(b, item[1])(*item[2], **item[3])
                elif item[0] == "inc":
                    ins.then_inc(item[1].h, item[2])
                else:
                    b.wait_ge(item[1].h, item[2])


class Sem:
    def __init__(self, stack, nc, name):
        self.h = stack.enter_context(nc.semaphore(name))
        self.n = 0


class Prog:
    def __init__(self, nc, stack):
        self.nc = nc
        self.stack = stack
        self.va = Sem(stack, nc, "auto_v")
        self.sa = Sem(stack, nc, "auto_s")
        self.sy = _Rec("sync")
        self.gp = _Rec("gpsimd")
        self.te = _Rec("tensor")
        self.ve = _Rec("vector", auto=self.va)
        self.sc = _Rec("scalar", auto=self.sa)

    def sem(self, name):
        return Sem(self.stack, self.nc, name)

    def emit(self):
        with self.nc.Block() as blk:
            @blk.sync
            def _(b):
                self.sy.replay(b)

            @blk.gpsimd
            def _(b):
                self.gp.replay(b)

            @blk.tensor
            def _(b):
                self.te.replay(b)

            @blk.vector
            def _(b):
                self.ve.replay(b)

            @blk.scalar
            def _(b):
                self.sc.replay(b)


def decl_in(nc, name, shape, dtype=F32):
    return nc.dram_tensor(name, list(shape), dtype, kind="ExternalInput").ap()


def decl_out(nc, name, shape, dtype=F32):
    return nc.dram_tensor(name, list(shape), dtype, kind="ExternalOutput").ap()



def emit_ln(p, C, xres, xn_sb, tag, u_dram=None, eps=1e-5, xld=None, post=None, ln_T=T):
    nc = p.nc
    _lst = ExitStack()
    E = _lst.enter_context
    LC = 512
    NLC = ln_T // LC
    if xld is None:
        xld = E(nc.sbuf_tensor(tag + "_xld", [128, 8 * LC], F32))
    sqb = E(nc.sbuf_tensor(tag + "_sq", [128, 2 * LC], F32))
    rowb = E(nc.sbuf_tensor(tag + "_row", [1, 4 * LC], F32))
    rbs = E(nc.sbuf_tensor(tag + "_rbs", [128, 2 * LC], F32))
    tmp = E(nc.sbuf_tensor(tag + "_tmp", [128, LC], F32))
    uo = E(nc.sbuf_tensor(tag + "_uo", [128, 2 * 8 * LC], F32)) if u_dram is not None else None
    if post is not None:
        gcol, bcol, xres2, xout2 = post
        xr2 = E(nc.sbuf_tensor(tag + "_xr2", [128, 8 * LC], F32))
        s_r = p.sem(tag + "_r"); s_po = p.sem(tag + "_po")
    psC, psD, psB = C["psC"], C["psD"], C["psB"]
    onec, onerow = C["onec"], C["onerow"]
    s_x = p.sem(tag + "_x"); s_t = p.sem(tag + "_t"); s_u = p.sem(tag + "_u")
    sq_mark = {}
    sqcons_mark = {}

    def xt(lc, ct):
        return xld[:, ct * LC: (ct + 1) * LC]

    vdone = 0
    for lc in range(NLC):
        pp = lc % 2
        if lc >= 1:
            p.sy.wait(p.va, vdone)
        p.sy.dma_start(
            out=xld[:, 0:8 * LC].rearrange("p (ct t) -> p ct t", ct=8),
            in_=xres.rearrange("(ct p) t -> p ct t", p=128)
                [:, :, lc * LC:(lc + 1) * LC]).inc(s_x, 16)
        p.sc.wait(s_x)
        p.te.wait(s_x)
        if lc >= 1:
            p.te.wait(p.va, vdone)
        for ct in range(8):
            p.te.matmul(psC[:1, 0:LC], onec[:], xt(lc, ct),
                        start=(ct == 0), stop=(ct == 7))
        for ct in range(8):
            k = lc * 8 + ct
            sq = sqb[:, (k % 2) * LC:(k % 2 + 1) * LC]
            if k >= 2:
                p.sc.wait(s_t, sqcons_mark[k - 2])
            p.sc.activation(out=sq, in_=xt(lc, ct), func=AF.Square)
            sq_mark[k] = p.sa.n
            p.te.wait(p.sa, sq_mark[k])
            p.te.matmul(psD[:1, 0:LC], onec[:], sq,
                        start=(ct == 0), stop=(ct == 7)).inc(s_t)
            sqcons_mark[k] = s_t.n
        rb = pp * 2 * LC
        mean = rowb[:1, rb:rb + LC]
        var = rowb[:1, rb + LC:rb + 2 * LC]
        p.ve.wait(s_t)
        p.ve.tensor_scalar(out=mean, in0=psC[:1, 0:LC], scalar1=1.0 / D,
                           scalar2=None, op0=ALU.mult)
        p.ve.tensor_scalar(out=var, in0=psD[:1, 0:LC], scalar1=1.0 / D,
                           scalar2=float(eps), op0=ALU.mult, op1=ALU.add)
        p.ve.tensor_tensor(out=tmp[:1, 0:LC], in0=mean, in1=mean, op=ALU.mult)
        p.ve.tensor_sub(var, var, tmp[:1, 0:LC])
        rows_v = p.va.n
        p.sc.wait(p.va, rows_v)
        p.sc.activation(out=var, in_=var, func=AF.Ln)
        p.sc.activation(out=var, in_=var, func=AF.Exp, scale=-0.5)
        rstd_s = p.sa.n
        p.te.wait(p.sa, rstd_s)
        p.te.wait(p.va, vdone)
        p.te.matmul(psB[:, 0:LC], onerow[:], mean, start=True, stop=True)
        p.te.matmul(psB[:, LC:2 * LC], onerow[:], var, start=True, stop=True).inc(s_t)
        if post is not None:
            if lc >= 1:
                p.sy.wait(s_po, lc * 16)
                p.sy.wait(p.va, vdone)
            p.sy.dma_start(
                out=xr2[:, 0:8 * LC].rearrange("p (ct t) -> p ct t", ct=8),
                in_=xres2.rearrange("(ct p) t -> p ct t", p=128)
                    [:, :, lc * LC:(lc + 1) * LC]).inc(s_r, 16)
        p.ve.wait(s_t)
        rbt = rbs[:, pp * LC:(pp + 1) * LC]
        p.ve.tensor_copy(rbt, psB[:, LC:2 * LC])
        if u_dram is not None and lc >= 2:
            p.ve.wait(s_u, (lc - 1) * 16)
        for ct in range(8):
            p.ve.tensor_sub(tmp[:], xt(lc, ct), psB[:, 0:LC])
            if u_dram is not None:
                ut = uo[:, pp * 8 * LC + ct * LC: pp * 8 * LC + (ct + 1) * LC]
                p.ve.tensor_mul(ut, tmp[:], rbt)
                if xn_sb is not None:
                    p.ve.tensor_copy(
                        xn_sb[:, ct * ln_T + lc * LC: ct * ln_T + (lc + 1) * LC], ut)
            elif post is not None:
                if ct == 0:
                    p.ve.wait(s_r)
                xr = xr2[:, ct * LC:(ct + 1) * LC]
                p.ve.tensor_mul(tmp[:], tmp[:], rbt)
                p.ve.tensor_scalar(out=tmp[:], in0=tmp[:],
                                   scalar1=gcol[:, ct:ct + 1],
                                   scalar2=bcol[:, ct:ct + 1],
                                   op0=ALU.mult, op1=ALU.add)
                p.ve.tensor_add(xr, xr, tmp[:])
            else:
                p.ve.tensor_mul(
                    xn_sb[:, ct * ln_T + lc * LC: ct * ln_T + (lc + 1) * LC],
                    tmp[:], rbt)
        vdone = p.va.n
        if post is not None:
            p.sy.wait(p.va, vdone)
            p.sy.dma_start(
                out=xout2.rearrange("(ct p) t -> p ct t", p=128)
                    [:, :, lc * LC:(lc + 1) * LC],
                in_=xr2[:, 0:8 * LC].rearrange("p (ct t) -> p ct t", ct=8)).inc(s_po, 16)
        if u_dram is not None:
            p.sy.wait(p.va, vdone)
            p.sy.dma_start(
                out=u_dram.rearrange("(ct p) t -> p ct t", p=128)
                    [:, :, lc * LC:(lc + 1) * LC],
                in_=uo[:, pp * 8 * LC:(pp + 1) * 8 * LC].rearrange(
                    "p (ct t) -> p ct t", ct=8)).inc(s_u, 16)
    if u_dram is not None:
        p.sy.wait(s_u)
    if post is not None:
        p.sy.wait(s_po)
    _lst.close()
    return dict(v_done=vdone, s_u=s_u, s_t=s_t, s_done=p.sa.n)


def first_argmax(p, m_out, src, rmax, scr):
    c0 = scr[:, 0:8]; c1 = scr[:, 8:16]; w8 = scr[:, 16:24]
    p.ve.reduce_max(rmax, src, axis=AX.X)
    p.ve.tensor_scalar(out=c0, in0=src, scalar1=rmax, scalar2=None,
                       op0=ALU.is_equal)
    p.ve.tensor_copy(c1[:, 0:1], c0[:, 0:1])
    p.ve.tensor_add(c1[:, 1:8], c0[:, 1:8], c0[:, 0:7])
    p.ve.tensor_copy(w8[:, 0:2], c1[:, 0:2])
    p.ve.tensor_add(w8[:, 2:8], c1[:, 2:8], c1[:, 0:6])
    p.ve.tensor_copy(c1[:, 0:4], w8[:, 0:4])
    p.ve.tensor_add(c1[:, 4:8], w8[:, 4:8], w8[:, 0:4])
    p.ve.tensor_sub(c1, c1, c0)
    p.ve.tensor_scalar(out=c1, in0=c1, scalar1=0.0, scalar2=None,
                       op0=ALU.is_equal)
    p.ve.tensor_mul(m_out, c0, c1)



def build_l0ffn(nc, dbg=False):
    xres = decl_in(nc, "xres", (D, T))
    xout = decl_out(nc, "xout", (D, T))
    w1 = decl_in(nc, "l0_w1t", (8, 128, 16, 2, 8, 128))
    w2 = decl_in(nc, "l0_w2t", (8, 128, 16, 8, 128))
    wr = decl_in(nc, "l0_wr", (128, 8, 1, 8))
    ones = decl_in(nc, "ones_col", (128, 1))
    onesr = decl_in(nc, "ones_row", (1, 128))
    ident = decl_in(nc, "ident", (128, 128))
    iota8 = decl_in(nc, "iota8", (128, 8))

    TC = 256
    NTC = T // TC
    acc_dram = nc.dram_tensor("acc_dram", [8, D, T], F32,
                              kind="ExternalOutput" if dbg else "Internal").ap()
    if dbg:
        dbg_xn = nc.dram_tensor("dbg_xn", [D, T], BF16, kind="ExternalOutput").ap()
        dbg_rw = nc.dram_tensor("dbg_rw", [8, T], F32, kind="ExternalOutput").ap()

    with ExitStack() as st:
        E = st.enter_context
        p = Prog(nc, st)
        xn = E(nc.sbuf_tensor("xn", [128, 8 * T], BF16))
        stag = E(nc.sbuf_tensor("stag", [128, 2048], F32))
        accld = E(nc.sbuf_tensor("accld", [128, 8 * 512], F32))
        w1sb = E(nc.sbuf_tensor("w1sb", [128, 16 * 2 * 8 * 128], BF16))
        w2sb = E(nc.sbuf_tensor("w2sb", [128, 16 * 8 * 128], BF16))
        onec = E(nc.sbuf_tensor("onec", [128, 1], F32))
        onerow = E(nc.sbuf_tensor("onerow", [1, 128], F32))
        idc = E(nc.sbuf_tensor("idc", [128, 128], F32))
        io8 = E(nc.sbuf_tensor("io8", [128, 8], F32))
        wrs = E(nc.sbuf_tensor("wrs", [128, 8 * 8], BF16))
        lgt = E(nc.sbuf_tensor("lgt", [128, 8], F32))
        gsc = E(nc.sbuf_tensor("gsc", [128, 64], F32))
        rwT = E(nc.sbuf_tensor("rwT", [8, T], F32))
        rbc = E(nc.sbuf_tensor("rbc", [128, TC], BF16))
        rwrow = E(nc.sbuf_tensor("rwrow", [1, T], F32))
        tv = E(nc.sbuf_tensor("tv", [128, 2 * TC], BF16))
        sgb = E(nc.sbuf_tensor("sgb", [128, 2 * TC], BF16))
        actb = E(nc.sbuf_tensor("actb", [128, 2 * TC], BF16))
        xldb = E(nc.sbuf_tensor("xldb", [128, 8 * 512], F32))
        psA = E(nc.psum_tensor("psA", [128, 2048], F32))
        psB = E(nc.psum_tensor("psB", [128, 1024], F32))
        psC = E(nc.psum_tensor("psC", [128, 512], F32))
        psD = E(nc.psum_tensor("psD", [128, 512], F32))
        C = dict(psC=psC, psD=psD, psB=psB, onec=onec, onerow=onerow)

        s_c = p.sem("s_c"); s_w = p.sem("s_w"); s_t = p.sem("s_t")
        s_f = p.sem("s_f"); s_x2 = p.sem("s_x2"); s_o = p.sem("s_o")

        p.gp.dma_start(out=onec[:], in_=ones).inc(s_c, 16)
        p.gp.dma_start(out=onerow[:], in_=onesr).inc(s_c, 16)
        p.gp.dma_start(out=idc[:], in_=ident).inc(s_c, 16)
        p.gp.dma_start(out=io8[:], in_=iota8).inc(s_c, 16)
        p.gp.dma_start(out=wrs[:], in_=wr.rearrange("p kt o n -> p (kt o n)")).inc(s_c, 16)
        p.gp.dma_start(out=w1sb[:], in_=w1[0].rearrange("p a b c d -> p (a b c d)")).inc(s_w, 16)
        p.gp.dma_start(out=w2sb[:], in_=w2[0].rearrange("p a b c -> p (a b c)")).inc(s_w, 16)
        p.te.wait(s_c); p.ve.wait(s_c); p.sc.wait(s_c)

        ln = emit_ln(p, C, xres, xn, tag="ln0", xld=xldb)

        p.te.wait(p.va, ln["v_done"])
        psC_free = 0
        for tt in range(16):
            p.te.wait(p.va, psC_free)
            for ct in range(8):
                tok = p.te.matmul(psC[:, 0:8],
                            xn[:, ct * T + tt * 128: ct * T + (tt + 1) * 128],
                            wrs[:, ct * 8:(ct + 1) * 8],
                            start=(ct == 0), stop=(ct == 7))
                if ct == 7:
                    tok.inc(s_t)
            p.ve.wait(s_t)
            p.ve.tensor_copy(lgt[:], psC[:, 0:8])
            psC_free = p.va.n
            mx = gsc[:, 56:57]; raw = gsc[:, 40:48]
            p.ve.reduce_max(mx, lgt[:], axis=AX.X, negate=True)
            p.sc.wait(p.va)
            p.sc.activation(out=raw, in_=lgt[:], func=AF.Exp, bias=mx[:, 0:1])
            p.ve.wait(p.sa)
            m1 = gsc[:, 0:8]; m2 = gsc[:, 8:16]
            g1 = gsc[:, 57:58]; g2 = gsc[:, 58:59]
            a1 = gsc[:, 60:61]; a2 = gsc[:, 61:62]
            first_argmax(p, m1, raw, g1, gsc[:, 16:40])
            w8 = gsc[:, 48:56]
            p.ve.tensor_scalar(out=w8, in0=m1, scalar1=-1.0, scalar2=1.0,
                               op0=ALU.mult, op1=ALU.add)
            p.ve.tensor_mul(w8, w8, raw)
            first_argmax(p, m2, w8, g2, gsc[:, 16:40])
            den = gsc[:, 59:60]
            p.ve.tensor_add(den, g1, g2)
            p.ve.reciprocal(den, den)
            p.ve.tensor_mul(a1, g1, den)
            p.ve.tensor_mul(a2, g2, den)
            rt = gsc[:, 24:32]
            p.ve.tensor_scalar(out=rt, in0=m1, scalar1=a1, scalar2=None,
                               op0=ALU.mult)
            p.ve.scalar_tensor_tensor(out=rt, in0=m2, scalar=a2, in1=rt,
                                      op0=ALU.mult, op1=ALU.add)
            p.te.wait(p.va)
            p.te.transpose(psD[0:8, 0:128], rt, idc[:]).inc(s_t)
            p.ve.wait(s_t)
            p.ve.tensor_copy(rwT[0:8, tt * 128:(tt + 1) * 128], psD[0:8, 0:128])
        router_done = p.va.n

        for e in range(8):
            if e > 0:
                p.gp.wait(s_t)
                p.gp.dma_start(out=w1sb[:],
                               in_=w1[e].rearrange("p a b c d -> p (a b c d)")).inc(s_w, 16)
                p.gp.dma_start(out=w2sb[:],
                               in_=w2[e].rearrange("p a b c -> p (a b c)")).inc(s_w, 16)
            else:
                p.gp.wait(p.va, router_done)
            p.gp.dma_start(out=rwrow[0:1, :], in_=rwT[e:e + 1, :]).inc(s_w, 16)
            p.te.wait(s_w)
            for tc in range(NTC):
                p.te.wait(p.va)
                p.te.wait(p.sa)
                p.te.matmul(psC[:, 0:TC], onerow[:],
                            rwrow[0:1, tc * TC:(tc + 1) * TC],
                            start=True, stop=True).inc(s_t)
                p.ve.wait(s_t)
                p.ve.tensor_copy(rbc[:], psC[:, 0:TC])
                rbc_mark = p.va.n
                out2_mark = {}
                vmark = {}
                smark = {}
                for j in range(16):
                    pp = j % 2
                    hv = psB[:, pp * 512: pp * 512 + TC]
                    hg = (psC if pp == 0 else psD)[:, 0:TC]
                    if j >= 2:
                        p.te.wait(p.va, vmark[j - 2])
                        p.te.wait(p.sa, smark[j - 2])
                    if j == 0:
                        p.te.wait(p.va, rbc_mark)
                    base = (j * 2) * 8 * 128
                    for ct in range(8):
                        p.te.matmul(hv, w1sb[:, base + ct * 128: base + (ct + 1) * 128],
                                    xn[:, ct * T + tc * TC: ct * T + (tc + 1) * TC],
                                    start=(ct == 0), stop=(ct == 7))
                    base = (j * 2 + 1) * 8 * 128
                    for ct in range(8):
                        tok = p.te.matmul(hg, w1sb[:, base + ct * 128: base + (ct + 1) * 128],
                                    xn[:, ct * T + tc * TC: ct * T + (tc + 1) * TC],
                                    start=(ct == 0), stop=(ct == 7))
                        if ct == 7:
                            tok.inc(s_t)
                    p.ve.wait(s_t)
                    p.ve.tensor_mul(tv[:, pp * TC:(pp + 1) * TC], hv, rbc[:])
                    vmark[j] = p.va.n
                    p.sc.wait(s_t)
                    p.sc.activation(out=sgb[:, pp * TC:(pp + 1) * TC], in_=hg,
                                    func=AF.Silu)
                    smark[j] = p.sa.n
                    p.ve.wait(p.sa, smark[j])
                    if j >= 2:
                        p.ve.wait(s_t, out2_mark[j - 2])
                    p.ve.tensor_mul(actb[:, pp * TC:(pp + 1) * TC],
                                    tv[:, pp * TC:(pp + 1) * TC],
                                    sgb[:, pp * TC:(pp + 1) * TC])
                    act_mark = p.va.n
                    p.te.wait(p.va, act_mark)
                    for ct in range(8):
                        tok = p.te.matmul(psA[:, ct * TC:(ct + 1) * TC],
                                    w2sb[:, (j * 8 + ct) * 128:(j * 8 + ct + 1) * 128],
                                    actb[:, pp * TC:(pp + 1) * TC],
                                    start=(j == 0 and ct % 2 == 0), stop=(j == 15))
                        if ct == 7:
                            tok.inc(s_t)
                    out2_mark[j] = s_t.n
                p.ve.wait(s_t)
                if e * NTC + tc >= 1:
                    p.ve.wait(s_f, (e * NTC + tc) * 16)
                for ct in range(8):
                    p.ve.tensor_copy(stag[:, ct * TC:(ct + 1) * TC],
                                     psA[:, ct * TC:(ct + 1) * TC])
                p.sy.wait(p.va)
                p.sy.dma_start(
                    out=acc_dram[e].rearrange("(ct p) t -> p ct t", p=128)
                        [:, :, tc * TC:(tc + 1) * TC],
                    in_=stag.ap().rearrange("p (ct t) -> p ct t", ct=8)).inc(s_f, 16)

        p.sy.wait(s_f)
        for oc in range(4):
            if oc >= 1:
                p.sy.wait(s_o, oc * 16)
            p.sy.dma_start(
                out=xldb[:, 0:8 * 512].rearrange("p (ct t) -> p ct t", ct=8),
                in_=xres.rearrange("(ct p) t -> p ct t", p=128)
                    [:, :, oc * 512:(oc + 1) * 512]).inc(s_x2, 16)
            for ct in range(8):
                k = oc * 8 + ct
                if k >= 1:
                    p.sy.wait(p.va)
                p.sy.dma_start(
                    out=accld[:, 0: 8 * 512].rearrange("p (e t) -> p e t", e=8),
                    in_=acc_dram[:, ct * 128:(ct + 1) * 128,
                                 oc * 512:(oc + 1) * 512].rearrange(
                                     "e p t -> p e t")).inc(s_x2, 16)
                p.ve.wait(s_x2)
                ab = accld[:, 0: 8 * 512]
                s0 = ab[:, 0:512]
                for e2 in range(1, 8):
                    p.ve.tensor_add(s0, s0, ab[:, e2 * 512:(e2 + 1) * 512])
                xo = xldb[:, ct * 512: (ct + 1) * 512]
                p.ve.scalar_tensor_tensor(
                    out=xo, in0=s0, scalar=0.5, in1=xo,
                    op0=ALU.mult, op1=ALU.add)
            p.sy.wait(p.va)
            p.sy.dma_start(
                out=xout.rearrange("(ct p) t -> p ct t", p=128)
                    [:, :, oc * 512:(oc + 1) * 512],
                in_=xldb[:, 0:8 * 512].rearrange("p (ct t) -> p ct t", ct=8)).inc(s_o, 16)
        if dbg:
            p.sy.wait(s_o)
            p.sy.dma_start(out=dbg_xn.rearrange("(ct p) t -> p ct t", p=128),
                           in_=xn.ap().rearrange("p (ct t) -> p ct t", ct=8)).inc(s_o, 16)
            p.sy.dma_start(out=dbg_rw, in_=rwT[:]).inc(s_o, 16)
        p.sy.wait(s_o)
        p.emit()
    return nc


def make_nc(builder, **kw):
    nc = bass.Bass("TRN2", target_bir_lowering=False)
    builder(nc, **kw)
    return nc


def make_l0ffn_nc(dbg=False):
    nc = bass.Bass("TRN2", target_bir_lowering=False)
    build_l0ffn(nc, dbg=dbg)
    return nc



def build_mb_pre(nc):
    xres = decl_in(nc, "xres", (D, T))
    u = decl_out(nc, "u", (D, T))
    xssm = decl_out(nc, "xssm", (D, T), BF16)
    xgate = decl_out(nc, "xgate", (D, T), BF16)
    wip = decl_in(nc, "mb_inprojT", (128, 8, 16, 128))
    bip = decl_in(nc, "mb_inproj_b", (128, 16))
    ones = decl_in(nc, "ones_col", (128, 1))
    onesr = decl_in(nc, "ones_row", (1, 128))
    with ExitStack() as st:
        E = st.enter_context
        p = Prog(nc, st)
        xn = E(nc.sbuf_tensor("xn", [128, 8 * T], BF16))
        wsb = E(nc.sbuf_tensor("wsb", [128, 8 * 16 * 128], BF16))
        bsb = E(nc.sbuf_tensor("bsb", [128, 16], F32))
        onec = E(nc.sbuf_tensor("onec", [128, 1], F32))
        onerow = E(nc.sbuf_tensor("onerow", [1, 128], F32))
        stg = E(nc.sbuf_tensor("stg", [128, 2 * 512], BF16))
        psA = E(nc.psum_tensor("psA", [128, 1024], F32))
        psB = E(nc.psum_tensor("psB", [128, 1024], F32))
        psC = E(nc.psum_tensor("psC", [128, 512], F32))
        psD = E(nc.psum_tensor("psD", [128, 512], F32))
        C = dict(psC=psC, psD=psD, psB=psB, onec=onec, onerow=onerow)
        s_c = p.sem("s_c"); s_w = p.sem("s_w"); s_t = p.sem("s_t")
        s_o = p.sem("s_o")
        p.gp.dma_start(out=onec[:], in_=ones).inc(s_c, 16)
        p.gp.dma_start(out=onerow[:], in_=onesr).inc(s_c, 16)
        p.gp.dma_start(out=bsb[:], in_=bip).inc(s_c, 16)
        p.gp.dma_start(out=wsb[:], in_=wip.rearrange("p a b c -> p (a b c)")).inc(s_w, 16)
        p.te.wait(s_c); p.ve.wait(s_c); p.sc.wait(s_c)
        ln = emit_ln(p, C, xres, xn, tag="mpl", u_dram=u)
        p.te.wait(p.va, ln["v_done"])
        p.te.wait(s_w)
        nst = 0
        for ft in range(16):
            for tc in range(4):
                pp = (ft * 4 + tc) % 2
                ps = psA[:, pp * 512:(pp + 1) * 512]
                if ft * 4 + tc >= 2:
                    p.te.wait(p.sa)
                for ct in range(8):
                    tok = p.te.matmul(ps, wsb[:, (ct * 16 + ft) * 128:(ct * 16 + ft + 1) * 128],
                                      xn[:, ct * T + tc * 512: ct * T + (tc + 1) * 512],
                                      start=(ct == 0), stop=(ct == 7))
                    if ct == 7:
                        tok.inc(s_t)
                p.sc.wait(s_t)
                if ft * 4 + tc >= 2:
                    p.sc.wait(s_o, (ft * 4 + tc - 1) * 16)
                p.sc.activation(out=stg[:, pp * 512:(pp + 1) * 512], in_=ps,
                                func=AF.Identity, bias=bsb[:, ft:ft + 1])
                smk = p.sa.n
                p.sy.wait(p.sa, smk)
                dst = xssm if ft < 8 else xgate
                fr = (ft % 8) * 128
                p.sy.dma_start(
                    out=dst.rearrange("(f p) t -> p f t", p=128)
                        [:, ft % 8: ft % 8 + 1, tc * 512:(tc + 1) * 512],
                    in_=stg[:, pp * 512:(pp + 1) * 512].unsqueeze(1)).inc(s_o, 16)
        p.sy.wait(s_o)
        p.emit()
    return nc


def build_mb_conv(nc):
    xssm = decl_in(nc, "xssm", (D, T), BF16)
    halo3 = decl_in(nc, "halo3", (D, 4), BF16)
    xconv = decl_out(nc, "xconv", (D, T), BF16)
    bc = decl_out(nc, "bc", (T, 32))
    dtT = decl_out(nc, "dtT", (D, T))
    cwc = decl_in(nc, "mb_convw_col", (128, 32))
    cbc = decl_in(nc, "mb_convb_col", (128, 8))
    bcw = decl_in(nc, "mb_bcT", (128, 8, 1, 32))
    bcb = decl_in(nc, "mb_bc_b", (1, 32))
    wdt = decl_in(nc, "mb_dtT", (128, 8, 8, 128))
    bdt = decl_in(nc, "mb_dt_b", (128, 8))
    ones = decl_in(nc, "ones_col", (128, 1))
    onesr = decl_in(nc, "ones_row", (1, 128))
    XW = 3 + T
    with ExitStack() as st:
        E = st.enter_context
        p = Prog(nc, st)
        xs = E(nc.sbuf_tensor("xs", [128, 8 * XW], BF16))
        xcv = E(nc.sbuf_tensor("xcv", [128, 8 * T], BF16))
        cacc = E(nc.sbuf_tensor("cacc", [128, T], F32))
        cw = E(nc.sbuf_tensor("cw", [128, 32], F32))
        cb = E(nc.sbuf_tensor("cb", [128, 8], F32))
        bcwsb = E(nc.sbuf_tensor("bcwsb", [128, 8 * 32], BF16))
        bcbbc = E(nc.sbuf_tensor("bcbbc", [128, 32], F32))
        bcsb = E(nc.sbuf_tensor("bcsb", [128, 16 * 32], F32))
        wdtsb = E(nc.sbuf_tensor("wdtsb", [128, 8 * 8 * 128], BF16))
        bdtsb = E(nc.sbuf_tensor("bdtsb", [128, 8], F32))
        rsb = E(nc.sbuf_tensor("rsb", [128, 2 * 512], F32))
        asb = E(nc.sbuf_tensor("asb", [128, 2 * 512], F32))
        dsb = E(nc.sbuf_tensor("dsb", [128, 2 * 512], F32))
        onec = E(nc.sbuf_tensor("onec", [128, 1], F32))
        onerow = E(nc.sbuf_tensor("onerow", [1, 128], F32))
        psA = E(nc.psum_tensor("psA", [128, 1024], F32))
        psC = E(nc.psum_tensor("psC", [128, 512], F32))
        s_c = p.sem("s_c"); s_x = p.sem("s_x"); s_t = p.sem("s_t")
        s_o = p.sem("s_o"); s_od = p.sem("s_od")
        p.gp.dma_start(out=cw[:], in_=cwc).inc(s_c, 16)
        p.gp.dma_start(out=cb[:], in_=cbc).inc(s_c, 16)
        p.gp.dma_start(out=bcwsb[:], in_=bcw.rearrange("p a b c -> p (a b c)")).inc(s_c, 16)
        p.gp.dma_start(out=bcbbc[0:1, :], in_=bcb).inc(s_c, 16)
        p.gp.dma_start(out=wdtsb[:], in_=wdt.rearrange("p a b c -> p (a b c)")).inc(s_c, 16)
        p.gp.dma_start(out=bdtsb[:], in_=bdt).inc(s_c, 16)
        p.gp.dma_start(out=onec[:], in_=ones).inc(s_c, 16)
        p.gp.dma_start(out=onerow[:], in_=onesr).inc(s_c, 16)
        for ct in range(8):
            p.sy.dma_start(out=xs[:, ct * XW: ct * XW + 3],
                           in_=halo3.rearrange("(f p) t -> p f t", p=128)
                           [:, ct, 1:4]).inc(s_x, 16)
            p.sy.dma_start(out=xs[:, ct * XW + 3: (ct + 1) * XW],
                           in_=xssm.rearrange("(f p) t -> p f t", p=128)
                           [:, ct, :]).inc(s_x, 16)
        p.ve.wait(s_c); p.ve.wait(s_x)
        p.sc.wait(s_c)
        for ct in range(8):
            w = ct * XW
            p.ve.tensor_scalar(out=cacc[:], in0=xs[:, w:w + T],
                               scalar1=cw[:, ct * 4:ct * 4 + 1], scalar2=None,
                               op0=ALU.mult)
            for k in range(1, 4):
                p.ve.scalar_tensor_tensor(out=cacc[:], in0=xs[:, w + k:w + k + T],
                                          scalar=cw[:, ct * 4 + k:ct * 4 + k + 1],
                                          in1=cacc[:], op0=ALU.mult, op1=ALU.add)
            p.sc.wait(p.va)
            p.sc.activation(out=xcv[:, ct * T:(ct + 1) * T], in_=cacc[:],
                            func=AF.Silu, bias=cb[:, ct:ct + 1])
            p.ve.wait(p.sa)
            p.sy.wait(p.sa)
            p.sy.dma_start(out=xconv.rearrange("(f p) t -> p f t", p=128)
                           [:, ct:ct + 1, :],
                           in_=xcv[:, ct * T:(ct + 1) * T].unsqueeze(1)).inc(s_od, 16)
        xcv_done = p.sa.n
        p.te.wait(s_c)
        p.te.wait(p.sa, xcv_done)
        p.te.matmul(psC[:, 0:32], onerow[:], bcbbc[0:1, :], start=True, stop=True).inc(s_t)
        p.ve.wait(s_t)
        p.ve.tensor_copy(bcbbc[:, 0:32], psC[:, 0:32])
        bb_mark = p.va.n
        for tt in range(16):
            p.te.wait(p.va)
            for ct in range(8):
                tok = p.te.matmul(psC[:, 0:32],
                                  xcv[:, ct * T + tt * 128: ct * T + (tt + 1) * 128],
                                  bcwsb[:, ct * 32:(ct + 1) * 32],
                                  start=(ct == 0), stop=(ct == 7))
                if ct == 7:
                    tok.inc(s_t)
            p.ve.wait(s_t)
            p.ve.tensor_add(bcsb[:, tt * 32:(tt + 1) * 32], psC[:, 0:32],
                            bcbbc[:, 0:32])
        p.sy.wait(p.va)
        p.sy.dma_start(out=bc.rearrange("(tt p) n -> p tt n", p=128),
                       in_=bcsb.ap().rearrange("p (tt n) -> p tt n", tt=16)).inc(s_o, 16)
        for dct in range(8):
            for tc in range(4):
                k = dct * 4 + tc
                pp = k % 2
                ps = psA[:, pp * 512:(pp + 1) * 512]
                if k >= 2:
                    p.te.wait(p.va)
                for ct in range(8):
                    tok = p.te.matmul(ps, wdtsb[:, (ct * 8 + dct) * 128:(ct * 8 + dct + 1) * 128],
                                      xcv[:, ct * T + tc * 512: ct * T + (tc + 1) * 512],
                                      start=(ct == 0), stop=(ct == 7))
                    if ct == 7:
                        tok.inc(s_t)
                rs = rsb[:, pp * 512:(pp + 1) * 512]
                as_ = asb[:, pp * 512:(pp + 1) * 512]
                ds = dsb[:, pp * 512:(pp + 1) * 512]
                p.sc.wait(s_t)
                if k >= 2:
                    p.sc.wait(s_o, 16 + (k - 1) * 16)
                p.sc.activation(out=rs, in_=ps, func=AF.Relu, bias=bdtsb[:, dct:dct + 1])
                p.sc.activation(out=as_, in_=ps, func=AF.Abs, bias=bdtsb[:, dct:dct + 1])
                p.sc.activation(out=as_, in_=as_, func=AF.Exp, scale=-1.0)
                p.sc.activation(out=as_, in_=as_, func=AF.Ln, bias=1.0)
                p.ve.wait(p.sa)
                p.ve.tensor_add(ds, rs, as_)
                p.sy.wait(p.va)
                p.sy.dma_start(out=dtT.rearrange("(f p) t -> p f t", p=128)
                               [:, dct:dct + 1, tc * 512:(tc + 1) * 512],
                               in_=ds.unsqueeze(1)).inc(s_o, 16)
        p.sy.wait(s_o)
        p.sy.wait(s_od)
        p.emit()
    return nc


def build_mb_scan(nc, dbg=False):
    T4 = 4096
    TCK = 128
    NCK = T4 // TCK
    xch_h = nc.dram_tensor("xch", [512, T4], BF16, kind="ExternalInput")
    dth_h = nc.dram_tensor("dth", [512, T4], F32, kind="ExternalInput")
    bf_h = nc.dram_tensor("bf", [16, T4], F32, kind="ExternalInput")
    cf_h = nc.dram_tensor("cf", [16, T4], F32, kind="ExternalInput")
    alog = decl_in(nc, "alogh", (128, 64))
    onesr = decl_in(nc, "ones_row", (1, 128))
    yh = nc.dram_tensor("yh", [512, T4], BF16, kind="ExternalOutput")
    if dbg:
        d_dA = nc.dram_tensor("d_dA", [128, 8192], F32, kind="ExternalOutput").ap()
        d_G = nc.dram_tensor("d_G", [128, 8192], F32, kind="ExternalOutput").ap()
        d_h = nc.dram_tensor("d_h", [128, 8192], F32, kind="ExternalOutput").ap()

    xch, dth, yh_ap = xch_h.ap(), dth_h.ap(), yh.ap()
    bf_ap, cf_ap = bf_h.ap(), cf_h.ap()
    with ExitStack() as st:
        E = st.enter_context
        p = Prog(nc, st)
        xc = E(nc.sbuf_tensor("xc", [128, 2 * 512], BF16))
        dt = E(nc.sbuf_tensor("dt", [128, 2 * 512], F32))
        brow = E(nc.sbuf_tensor("brow", [1, 2 * 2048], F32))
        crow = E(nc.sbuf_tensor("crow", [1, 2 * 2048], F32))
        onerow = E(nc.sbuf_tensor("onerow", [1, 128], F32))
        bbps = E(nc.psum_tensor("bbps", [128, 2048], F32))
        cbps = E(nc.psum_tensor("cbps", [128, 2048], F32))
        dA = E(nc.sbuf_tensor("dA", [128, 8192], F32))
        G = E(nc.sbuf_tensor("G", [128, 8192], F32))
        hb = E(nc.sbuf_tensor("hb", [128, 2 * 8192], F32))
        dtx = E(nc.sbuf_tensor("dtx", [128, 512], F32))
        ys = E(nc.sbuf_tensor("ys", [128, 2 * 512], F32))
        acol = E(nc.sbuf_tensor("acol", [128, 64], F32))
        s_c = p.sem("s_c"); s_x = p.sem("s_x"); s_y = p.sem("s_y")
        s_t = p.sem("s_t")
        p.gp.dma_start(out=acol[:], in_=alog).inc(s_c, 16)
        p.gp.dma_start(out=onerow[:], in_=onesr).inc(s_c, 16)
        p.sc.wait(s_c)
        p.sc.activation(out=acol[:], in_=acol[:], func=AF.Exp)
        p.ve.wait(p.sa)
        p.ve.tensor_scalar(out=acol[:], in0=acol[:], scalar1=-1.0, scalar2=None,
                           op0=ALU.mult)
        aprep = p.va.n
        xmark = {}
        vmark = {}
        for ck in range(NCK):
            pp = ck % 2
            t0 = ck * TCK
            if ck >= 2:
                p.sy.wait(p.va, vmark[ck - 2])
            p.sy.dma_start(out=xc[:, pp * 512:(pp + 1) * 512]
                           .rearrange("p (dh t) -> p dh t", dh=4),
                           in_=xch.rearrange("(dh p) t -> p dh t", p=128)
                           [:, :, t0:t0 + TCK]).inc(s_x, 16)
            p.sy.dma_start(out=dt[:, pp * 512:(pp + 1) * 512]
                           .rearrange("p (dh t) -> p dh t", dh=4),
                           in_=dth.rearrange("(dh p) t -> p dh t", p=128)
                           [:, :, t0:t0 + TCK]).inc(s_x, 16)
            p.sy.dma_start(out=brow[0:1, pp * 2048:(pp + 1) * 2048]
                           .rearrange("p (n t) -> p n t", n=16),
                           in_=bf_ap[:, t0:t0 + TCK].unsqueeze(0)).inc(s_x, 16)
            p.sy.dma_start(out=crow[0:1, pp * 2048:(pp + 1) * 2048]
                           .rearrange("p (n t) -> p n t", n=16),
                           in_=cf_ap[:, t0:t0 + TCK].unsqueeze(0)).inc(s_x, 16)
            xmark[ck] = s_x.n
            p.te.wait(s_x, xmark[ck])
            if ck >= 1:
                p.te.wait(p.va, vmark[ck - 1])
            for q in range(4):
                p.te.matmul(bbps[:, q * 512:(q + 1) * 512], onerow[:],
                            brow[0:1, pp * 2048 + q * 512: pp * 2048 + (q + 1) * 512],
                            start=True, stop=True)
            for q in range(4):
                tok = p.te.matmul(cbps[:, q * 512:(q + 1) * 512], onerow[:],
                                  crow[0:1, pp * 2048 + q * 512: pp * 2048 + (q + 1) * 512],
                                  start=True, stop=True)
                if q == 3:
                    tok.inc(s_t)
            p.ve.wait(s_x, xmark[ck])
            p.ve.wait(s_t)
            if ck == 0:
                p.ve.wait(p.va, aprep)
            dtp = dt[:, pp * 512:(pp + 1) * 512]
            xcp = xc[:, pp * 512:(pp + 1) * 512]

            p.ve.tensor_tensor(
                out=dA.ap().rearrange("p (dh n t) -> p dh n t", dh=4, n=16),
                in0=bass.AP(dt.tensor, dt_off := (pp * 512) * 4, [[0, 0]]) if False else
                    dtp.rearrange("p (dh t) -> p dh t", dh=4).unsqueeze(2)
                    .broadcast_to([128, 4, 16, TCK]),
                in1=acol.ap().rearrange("p (dh n) -> p dh n", dh=4).unsqueeze(3)
                    .broadcast_to([128, 4, 16, TCK]),
                op=ALU.mult)
            p.sc.wait(p.va)
            p.sc.activation(out=dA[:], in_=dA[:], func=AF.Exp)
            p.ve.tensor_tensor(out=dtx[:], in0=dtp, in1=xcp, op=ALU.mult)
            p.ve.tensor_tensor(
                out=G.ap().rearrange("p (dh n t) -> p dh n t", dh=4, n=16),
                in0=dtx.ap().rearrange("p (dh t) -> p dh t", dh=4).unsqueeze(2)
                    .broadcast_to([128, 4, 16, TCK]),
                in1=bbps.ap().rearrange("p (n t) -> p n t", n=16).unsqueeze(1)
                    .broadcast_to([128, 4, 16, TCK]),
                op=ALU.mult)
            p.ve.wait(p.sa)
            hcur = hb[:, pp * 8192:(pp + 1) * 8192]
            hprev = hb[:, (1 - pp) * 8192:(2 - pp) * 8192]
            for dh in range(4):
                for n in range(16):
                    off = (dh * 16 + n) * TCK
                    if ck == 0:
                        ini = 0.0
                    else:
                        ini = hprev[:, off + TCK - 1: off + TCK]
                    p.ve.tensor_tensor_scan(
                        out=hcur[:, off:off + TCK],
                        data0=dA[:, off:off + TCK],
                        data1=G[:, off:off + TCK],
                        initial=ini, op0=ALU.mult, op1=ALU.add)
            p.ve.tensor_tensor(
                out=G.ap().rearrange("p (dh n t) -> p dh n t", dh=4, n=16),
                in0=hcur.rearrange("p (dh n t) -> p dh n t", dh=4, n=16),
                in1=cbps.ap().rearrange("p (n t) -> p n t", n=16).unsqueeze(1)
                    .broadcast_to([128, 4, 16, TCK]),
                op=ALU.mult)
            if ck >= 2:
                p.ve.wait(s_y, (ck - 1) * 16)
            p.ve.tensor_reduce(
                out=ys[:, pp * 512:(pp + 1) * 512].rearrange("p (dh t) -> p dh t", dh=4),
                in_=G.ap().rearrange("p (dh n t) -> p dh t n", dh=4, n=16),
                axis=AX.X, op=ALU.add)
            vmark[ck] = p.va.n
            if dbg and ck == 1:
                p.gp.wait(p.sa)
                p.gp.dma_start(out=d_dA, in_=dA[:]).inc(s_y, 16)
                p.gp.wait(p.va, vmark[ck])
                p.gp.dma_start(out=d_h, in_=hb[:, 8192:16384]).inc(s_y, 16)
                p.gp.dma_start(out=d_G, in_=G[:]).inc(s_y, 16)
            p.gp.wait(p.va, vmark[ck])
            p.gp.dma_start(out=yh_ap.rearrange("(dh p) t -> p dh t", p=128)
                           [:, :, t0:t0 + TCK],
                           in_=ys[:, pp * 512:(pp + 1) * 512]
                           .rearrange("p (dh t) -> p dh t", dh=4)).inc(s_y, 16)
        p.gp.wait(s_y)
        p.emit()
    return nc


def build_mb_post(nc):
    y = decl_in(nc, "y", (D, T), BF16)
    xconv = decl_in(nc, "xconv", (D, T), BF16)
    xgate = decl_in(nc, "xgate", (D, T), BF16)
    u = decl_in(nc, "u", (D, T))
    xres = decl_in(nc, "xres", (D, T))
    xout = decl_out(nc, "xout", (D, T))
    wout = decl_in(nc, "mb_outT", (128, 8, 8, 128))
    bout = decl_in(nc, "mb_out_b", (128, 8))
    dcol = decl_in(nc, "mb_D_col", (128, 8))
    lng = decl_in(nc, "mb_lng", (128, 8))
    lnb = decl_in(nc, "mb_lnb", (128, 8))
    ones = decl_in(nc, "ones_col", (128, 1))
    onesr = decl_in(nc, "ones_row", (1, 128))
    z_dram = nc.dram_tensor("z_dram", [D, T], F32, kind="Internal").ap()
    with ExitStack() as st:
        E = st.enter_context
        p = Prog(nc, st)
        ysb = E(nc.sbuf_tensor("ysb", [128, 8 * 512], BF16))
        xcsb = E(nc.sbuf_tensor("xcsb", [128, 8 * 512], BF16))
        xgsb = E(nc.sbuf_tensor("xgsb", [128, 8 * 512], BF16))
        usb = E(nc.sbuf_tensor("usb", [128, 8 * 512], F32))
        yg = E(nc.sbuf_tensor("yg", [128, 8 * 512], BF16))
        t1 = E(nc.sbuf_tensor("t1", [128, 512], F32))
        sgs = E(nc.sbuf_tensor("sgs", [128, 512], BF16))
        zst = E(nc.sbuf_tensor("zst", [128, 8 * 512], F32))
        wsb = E(nc.sbuf_tensor("wsb", [128, 8 * 8 * 128], BF16))
        bsb = E(nc.sbuf_tensor("bsb", [128, 8], F32))
        dsb = E(nc.sbuf_tensor("dsb", [128, 8], F32))
        gsb = E(nc.sbuf_tensor("gsb", [128, 8], F32))
        bbsb = E(nc.sbuf_tensor("bbsb", [128, 8], F32))
        onec = E(nc.sbuf_tensor("onec", [128, 1], F32))
        onerow = E(nc.sbuf_tensor("onerow", [1, 128], F32))
        psA = E(nc.psum_tensor("psA", [128, 1024], F32))
        psB = E(nc.psum_tensor("psB", [128, 1024], F32))
        psC = E(nc.psum_tensor("psC", [128, 512], F32))
        psD = E(nc.psum_tensor("psD", [128, 512], F32))
        C = dict(psC=psC, psD=psD, psB=psB, onec=onec, onerow=onerow)
        s_c = p.sem("s_c"); s_x = p.sem("s_x"); s_t = p.sem("s_t")
        s_z = p.sem("s_z")
        for ap_, dst in [(ones, onec), (onesr, onerow), (bout, bsb), (dcol, dsb),
                         (lng, gsb), (lnb, bbsb)]:
            p.gp.dma_start(out=dst[:], in_=ap_).inc(s_c, 16)
        p.gp.dma_start(out=wsb[:], in_=wout.rearrange("p a b c -> p (a b c)")).inc(s_c, 16)
        p.ve.wait(s_c); p.sc.wait(s_c); p.te.wait(s_c)
        zmark = {}
        for tc in range(4):
            sl = (slice(None), slice(tc * 512, (tc + 1) * 512))
            if tc >= 1:
                p.sy.wait(p.va)
                p.sy.wait(p.sa)
                p.sy.wait(s_z, tc * 16)
            for nm, dr, sb in [("y", y, ysb), ("xc", xconv, xcsb), ("xg", xgate, xgsb),
                               ("u", u, usb)]:
                p.sy.dma_start(out=sb.ap().rearrange("p (f t) -> p f t", f=8),
                               in_=dr.rearrange("(f p) t -> p f t", p=128)
                               [:, :, tc * 512:(tc + 1) * 512]).inc(s_x, 16)
            p.ve.wait(s_x)
            p.sc.wait(s_x)
            if tc >= 1:
                p.ve.wait(s_t)
            for ct in range(8):
                cs = slice(ct * 512, (ct + 1) * 512)
                p.ve.tensor_scalar(out=t1[:], in0=xcsb[:, cs],
                                   scalar1=dsb[:, ct:ct + 1], scalar2=None,
                                   op0=ALU.mult)
                p.ve.tensor_add(t1[:], t1[:], ysb[:, cs])
                p.sc.wait(p.va)
                p.sc.activation(out=sgs[:], in_=xgsb[:, cs], func=AF.Silu)
                p.ve.wait(p.sa)
                p.ve.tensor_mul(yg[:, cs], t1[:], sgs[:])
            ygm = p.va.n
            p.te.wait(p.va, ygm)
            for co in range(8):
                pp = co % 2
                ps = psA[:, pp * 512:(pp + 1) * 512]
                if co >= 2:
                    p.te.wait(p.va)
                for ct in range(8):
                    tok = p.te.matmul(ps, wsb[:, (ct * 8 + co) * 128:(ct * 8 + co + 1) * 128],
                                      yg[:, ct * 512:(ct + 1) * 512],
                                      start=(ct == 0), stop=(ct == 7))
                    if ct == 7:
                        tok.inc(s_t)
                p.ve.wait(s_t)
                p.ve.tensor_scalar(out=zst[:, co * 512:(co + 1) * 512], in0=ps,
                                   scalar1=bsb[:, co:co + 1], scalar2=None,
                                   op0=ALU.add)
                p.ve.tensor_add(zst[:, co * 512:(co + 1) * 512],
                                zst[:, co * 512:(co + 1) * 512], usb[:, co * 512:(co + 1) * 512])
            p.sy.wait(p.va)
            p.sy.dma_start(out=z_dram.rearrange("(f p) t -> p f t", p=128)
                           [:, :, tc * 512:(tc + 1) * 512],
                           in_=zst.ap().rearrange("p (f t) -> p f t", f=8)).inc(s_z, 16)
        p.sy.wait(s_z)
        emit_ln(p, C, z_dram, None, tag="ml2", post=(gsb, bbsb, xres, xout))
        p.emit()
    return nc



def fence(p, s_t, dma_sems=()):
    p.te.wait(p.va); p.te.wait(p.sa)
    p.ve.wait(s_t); p.ve.wait(p.sa)
    p.sc.wait(s_t); p.sc.wait(p.va)
    for s in dma_sems:
        p.te.wait(s); p.ve.wait(s); p.sc.wait(s)


def build_attn(nc, dbg=False):
    TE = T + 512
    xres_ext = decl_in(nc, "xres_ext", (D, TE))
    xout = decl_out(nc, "xout", (D, T))
    wq = decl_in(nc, "at_wqT", (128, 8, 8, 128))
    wk = decl_in(nc, "at_wkT", (128, 8, 8, 128))
    wv = decl_in(nc, "at_wvT", (128, 8, 2, 512))
    wo = decl_in(nc, "at_woT", (128, 8, 8, 128))
    qnw = decl_in(nc, "at_qnw", (128, 8))
    knw = decl_in(nc, "at_knw", (128, 8))
    ones = decl_in(nc, "ones_col", (128, 1))
    onesr = decl_in(nc, "ones_row", (1, 128))
    ident = decl_in(nc, "ident", (128, 128))
    mask0_in = decl_in(nc, "mask0", (128, 640), BF16)
    edge_in = decl_in(nc, "edge_mask", (128, 4 * 512), BF16)
    alibi_in = decl_in(nc, "alibi_flat", (16, 640))
    NHC = TE // 512
    if dbg:
        d_qn = nc.dram_tensor("d_qn", [D, T], BF16, kind="ExternalOutput").ap()
        d_kn = nc.dram_tensor("d_kn", [D, TE], BF16, kind="ExternalOutput").ap()
        d_vt = nc.dram_tensor("d_vt", [20 * 128, 1024], BF16, kind="ExternalOutput").ap()
        d_xn = nc.dram_tensor("d_xn", [D, TE], BF16, kind="ExternalOutput").ap()
        d_oa = nc.dram_tensor("d_oa", [D, T], BF16, kind="ExternalOutput").ap()

    with ExitStack() as st:
        E = st.enter_context
        p = Prog(nc, st)
        onec = E(nc.sbuf_tensor("onec", [128, 1], F32))
        onerow = E(nc.sbuf_tensor("onerow", [1, 128], F32))
        idc = E(nc.sbuf_tensor("idc", [128, 128], BF16))
        msk = E(nc.sbuf_tensor("msk", [128, 640], BF16))
        edg = E(nc.sbuf_tensor("edg", [128, 4 * 512], BF16))
        al16 = E(nc.sbuf_tensor("al16", [16, 640], F32))
        alb = E(nc.sbuf_tensor("alb", [1, 640], F32))
        qw = E(nc.sbuf_tensor("qw", [128, 8], F32))
        kw = E(nc.sbuf_tensor("kw", [128, 8], F32))
        rowq = E(nc.sbuf_tensor("rowq", [1, 512], F32))
        rsbq = E(nc.sbuf_tensor("rsbq", [128, 512], F32))
        sqs = E(nc.sbuf_tensor("sqs", [128, 2 * 512], F32))
        qf = E(nc.sbuf_tensor("qf", [128, 8 * 512], F32))
        qn = E(nc.sbuf_tensor("qn", [128, 8 * T], BF16))
        kn = E(nc.sbuf_tensor("kn", [128, 8 * TE], BF16))
        vtm = E(nc.sbuf_tensor("vtm", [128, 20 * 1024], BF16))
        psA = E(nc.psum_tensor("psA", [128, 1024], F32))
        psB = E(nc.psum_tensor("psB", [128, 1024], F32))
        psC = E(nc.psum_tensor("psC", [128, 512], F32))
        psD = E(nc.psum_tensor("psD", [128, 512], F32))
        psT = E(nc.psum_tensor("psT", [128, 1024], BF16))
        C = dict(psC=psC, psD=psD, psB=psB, onec=onec, onerow=onerow)
        s_c = p.sem("s_c"); s_t = p.sem("s_t"); s_o = p.sem("s_o")
        s_x = p.sem("s_x"); s_ab = p.sem("s_ab")

        for ap_, dst in [(ones, onec), (onesr, onerow), (mask0_in, msk),
                         (edge_in, edg), (alibi_in, al16), (qnw, qw), (knw, kw)]:
            p.gp.dma_start(out=dst[:], in_=ap_).inc(s_c, 16)
        p.gp.dma_start(out=idc[:], in_=ident).inc(s_c, 16)
        p.te.wait(s_c); p.ve.wait(s_c); p.sc.wait(s_c)

        def qk_chunk(wsb, wcol, hc, dst_tile_fn):
            base_t = hc * 512
            for dqt in range(8):
                pp = dqt % 2
                ps = psB[:, pp * 512:(pp + 1) * 512]
                if dqt >= 2:
                    p.te.wait(p.va)
                    p.te.wait(p.sa)
                for ct in range(8):
                    tok = p.te.matmul(ps, wsb[:, (ct * 8 + dqt) * 128:(ct * 8 + dqt + 1) * 128],
                                      xn[:, ct * TE + base_t: ct * TE + base_t + 512],
                                      start=(ct == 0), stop=(ct == 7))
                    if ct == 7:
                        tok.inc(s_t)
                p.ve.wait(s_t)
                p.ve.tensor_copy(qf[:, dqt * 512:(dqt + 1) * 512], ps)
                p.sc.wait(p.va)
                sq = sqs[:, pp * 512:(pp + 1) * 512]
                p.sc.activation(out=sq, in_=ps, func=AF.Square)
                p.te.wait(p.sa)
                p.te.matmul(psC[:1, 0:512], onec[:], sq,
                            start=(dqt == 0), stop=(dqt == 7)).inc(s_t)
            p.ve.wait(s_t)
            p.ve.tensor_scalar(out=rowq[:1, 0:512], in0=psC[:1, 0:512],
                               scalar1=1.0 / D, scalar2=1e-6,
                               op0=ALU.mult, op1=ALU.add)
            rmark = p.va.n
            p.sc.wait(p.va, rmark)
            p.sc.activation(out=rowq[:1, 0:512], in_=rowq[:1, 0:512], func=AF.Ln)
            p.sc.activation(out=rowq[:1, 0:512], in_=rowq[:1, 0:512],
                            func=AF.Exp, scale=-0.5)
            p.te.wait(p.sa)
            p.te.matmul(psD[:, 0:512], onerow[:], rowq[:1, 0:512],
                        start=True, stop=True).inc(s_t)
            p.ve.wait(s_t)
            p.ve.tensor_copy(rsbq[:], psD[:, 0:512])
            for dqt in range(8):
                p.ve.tensor_mul(qf[:, dqt * 512:(dqt + 1) * 512],
                                qf[:, dqt * 512:(dqt + 1) * 512], rsbq[:])
                p.ve.tensor_scalar(out=dst_tile_fn(dqt),
                                   in0=qf[:, dqt * 512:(dqt + 1) * 512],
                                   scalar1=wcol[:, dqt:dqt + 1], scalar2=None,
                                   op0=ALU.mult)

        xn_cm = nc.sbuf_tensor("xn", [128, 8 * TE], BF16)
        xn = xn_cm.__enter__()
        ln = emit_ln(p, C, xres_ext, xn, tag="lna", ln_T=TE, xld=qf)
        p.te.wait(p.va, ln["v_done"])
        if dbg:
            s_dx = p.sem("s_dx")
            p.sy.wait(p.va, ln["v_done"])
            p.sy.dma_start(out=d_xn.rearrange("(a p) t -> p a t", p=128),
                           in_=xn.ap().rearrange("p (a t) -> p a t", a=8)).inc(s_dx, 16)
            p.sy.wait(s_dx)
        p.gp.wait(p.va, ln["v_done"])
        p.gp.wait(ln["s_t"])
        p.gp.wait(p.sa)
        wqk_cm = nc.sbuf_tensor("wqk", [128, 8 * 8 * 128], BF16)
        wqk = wqk_cm.__enter__()
        s_w = p.sem("s_w")
        p.gp.dma_start(out=wqk[:], in_=wq.rearrange("p a b c -> p (a b c)")).inc(s_w, 16)
        p.te.wait(s_w)
        for hc in range(4):
            qk_chunk(wqk, qw, hc + 1,
                     lambda dqt, hc=hc: qn[:, dqt * T + hc * 512: dqt * T + (hc + 1) * 512])
        p.gp.wait(s_t)
        p.gp.dma_start(out=wqk[:], in_=wk.rearrange("p a b c -> p (a b c)")).inc(s_w, 16)
        p.te.wait(s_w)
        for hc in range(NHC):
            qk_chunk(wqk, kw, hc,
                     lambda dqt, hc=hc: kn[:, dqt * TE + hc * 512: dqt * TE + (hc + 1) * 512])
        p.gp.wait(s_t)
        p.gp.dma_start(out=wqk[:], in_=wv.rearrange("p a b c -> p (a b c)")).inc(s_w, 16)
        p.te.wait(s_w)
        wvs = wqk
        for tt in range(20):
            for dvc in range(2):
                pp = (tt * 2 + dvc) % 2
                ps = psB[:, pp * 512:(pp + 1) * 512]
                p.te.wait(p.va)
                for ct in range(8):
                    tok = p.te.matmul(ps, xn[:, ct * TE + tt * 128: ct * TE + (tt + 1) * 128],
                                      wvs[:, (ct * 2 + dvc) * 512:(ct * 2 + dvc + 1) * 512],
                                      start=(ct == 0), stop=(ct == 7))
                    if ct == 7:
                        tok.inc(s_t)
                p.ve.wait(s_t)
                p.ve.tensor_copy(vtm[:, tt * 1024 + dvc * 512: tt * 1024 + (dvc + 1) * 512], ps)
        fence(p, s_t)
        wqk_cm.__exit__(None, None, None)
        xn_cm.__exit__(None, None, None)

        oaT_cm = nc.sbuf_tensor("oaT", [128, 8 * T], BF16); oaT = oaT_cm.__enter__()
        mxc_cm = nc.sbuf_tensor("mxc", [128, 4], F32); mxc = mxc_cm.__enter__()
        rsc_cm = nc.sbuf_tensor("rsc", [128, 4], F32); rsc = rsc_cm.__enter__()
        rdc_cm = nc.sbuf_tensor("rdc", [128, 4], F32); rdc = rdc_cm.__enter__()
        scs_cm = nc.sbuf_tensor("scs", [128, 640], F32); scs = scs_cm.__enter__()
        pbf_cm = nc.sbuf_tensor("pbf", [128, 640], BF16); pbf = pbf_cm.__enter__()
        ptb_cm = nc.sbuf_tensor("ptb", [128, 640], BF16); ptb = ptb_cm.__enter__()
        oas_cm = nc.sbuf_tensor("oas", [128, 4 * 1024], BF16); oas = oas_cm.__enter__()
        alsc = 0.125
        for hc in range(4):
            for h in range(16):
                p.gp.wait(s_t)
                p.gp.dma_start(out=alb[0:1, :], in_=alibi_in[h:h + 1, :]).inc(s_ab, 16)
                p.te.wait(s_ab)
                for qbl in range(4):
                    qb = hc * 4 + qbl
                    sc1 = psA[:, 0:512]
                    sc2 = psA[:, 512:640]
                    lhs = qn[(h % 2) * 64:(h % 2) * 64 + 64,
                             (h // 2) * T + qb * 128:(h // 2) * T + (qb + 1) * 128]
                    if hc + h + qbl > 0:
                        p.te.wait(p.va)
                    kb0 = qb * 128
                    kslc = kn[(h % 2) * 64:(h % 2) * 64 + 64,
                              (h // 2) * TE + kb0: (h // 2) * TE + kb0 + 512]
                    p.te.matmul(sc1, lhs, kslc, start=True, stop=False)
                    p.te.matmul(sc1, onerow[:], alb[0:1, 0:512],
                                start=False, stop=True).inc(s_t)
                    kslc2 = kn[(h % 2) * 64:(h % 2) * 64 + 64,
                               (h // 2) * TE + kb0 + 512: (h // 2) * TE + kb0 + 640]
                    p.te.matmul(sc2, lhs, kslc2, start=True, stop=False)
                    p.te.matmul(sc2, onerow[:], alb[0:1, 512:640],
                                start=False, stop=True).inc(s_t)
                    p.ve.wait(s_t)
                    p.ve.scalar_tensor_tensor(out=scs[:, 0:512], in0=sc1,
                                              scalar=alsc, in1=msk[:, 0:512],
                                              op0=ALU.mult, op1=ALU.add)
                    p.ve.scalar_tensor_tensor(out=scs[:, 512:640], in0=sc2,
                                              scalar=alsc, in1=msk[:, 512:640],
                                              op0=ALU.mult, op1=ALU.add)
                    if qb < 4:
                        p.ve.tensor_add(scs[:, 0:512], scs[:, 0:512],
                                        edg[:, qb * 512: (qb + 1) * 512])
                    p.ve.reduce_max(mxc[:, 0:1], scs[:], axis=AX.X, negate=True)
                    vmk = p.va.n
                    p.sc.wait(p.va, vmk)
                    p.sc.activation(out=pbf[:], in_=scs[:], func=AF.Exp,
                                    bias=mxc[:, 0:1], accum_out=rsc[:, 0:1])
                    smk = p.sa.n
                    p.ve.wait(p.sa, smk)
                    p.ve.reciprocal(rdc[:, 0:1], rsc[:, 0:1])
                    p.te.wait(p.sa, smk)
                    p.te.wait(p.va)
                    for kb in range(5):
                        tok = p.te.transpose(psT[:, kb * 128:(kb + 1) * 128],
                                             pbf[:, kb * 128:(kb + 1) * 128], idc[:])
                        if kb == 4:
                            tok.inc(s_t)
                    p.ve.wait(s_t)
                    p.ve.tensor_copy(ptb[:], psT[:, 0:640])
                    ptk = p.va.n
                    p.te.wait(p.va, ptk)
                    pvp = qbl % 2
                    pv = psB[:, pvp * 512: pvp * 512 + 64]
                    for kb in range(5):
                        tok = p.te.matmul(pv, ptb[:, kb * 128:(kb + 1) * 128],
                                          vtm[:, (qb + kb) * 1024 + h * 64:
                                              (qb + kb) * 1024 + (h + 1) * 64],
                                          start=(kb == 0), stop=(kb == 4))
                        if kb == 4:
                            tok.inc(s_t)
                    p.ve.wait(s_t)
                    p.ve.tensor_scalar(out=oas[:, qbl * 1024 + h * 64: qbl * 1024 + (h + 1) * 64],
                                       in0=pv, scalar1=rdc[:, 0:1], scalar2=None,
                                       op0=ALU.mult)
            oam = p.va.n
            p.te.wait(p.va, oam)
            for qbl in range(4):
                qb = hc * 4 + qbl
                for ct in range(8):
                    tok = p.te.transpose(psT[:, ct * 128:(ct + 1) * 128],
                                         oas[:, qbl * 1024 + ct * 128: qbl * 1024 + (ct + 1) * 128],
                                         idc[:])
                    if ct == 7:
                        tok.inc(s_t)
                p.ve.wait(s_t)
                for ct in range(8):
                    p.ve.tensor_copy(oaT[:, ct * T + qb * 128: ct * T + (qb + 1) * 128],
                                     psT[:, ct * 128:(ct + 1) * 128])
                p.te.wait(p.va)
        fence(p, s_t)
        for cm in [oas_cm, ptb_cm, pbf_cm, scs_cm]:
            cm.__exit__(None, None, None)

        wos_cm = nc.sbuf_tensor("wos", [128, 8 * 8 * 128], BF16); wos = wos_cm.__enter__()
        xst = qf
        s_w2 = p.sem("s_w2")
        p.gp.wait(s_t)
        p.gp.dma_start(out=wos[:], in_=wo.rearrange("p a b c -> p (a b c)")).inc(s_w2, 16)
        p.te.wait(s_w2)
        for tc in range(4):
            if tc >= 1:
                p.sy.wait(s_o, tc * 16)
            p.sy.wait(p.va)
            p.sy.dma_start(
                out=xst[:, 0:8 * 512].rearrange("p (ct t) -> p ct t", ct=8),
                in_=xres_ext.rearrange("(ct p) t -> p ct t", p=128)
                    [:, :, 512 + tc * 512: 512 + (tc + 1) * 512]).inc(s_x, 16)
            p.ve.wait(s_x)
            for co in range(8):
                pp = co % 2
                ps = psA[:, pp * 512:(pp + 1) * 512]
                if co >= 2 or tc > 0:
                    p.te.wait(p.va)
                for ct in range(8):
                    tok = p.te.matmul(ps, wos[:, (ct * 8 + co) * 128:(ct * 8 + co + 1) * 128],
                                      oaT[:, ct * T + tc * 512: ct * T + (tc + 1) * 512],
                                      start=(ct == 0), stop=(ct == 7))
                    if ct == 7:
                        tok.inc(s_t)
                p.ve.wait(s_t)
                p.ve.tensor_add(xst[:, co * 512:(co + 1) * 512],
                                xst[:, co * 512:(co + 1) * 512], ps)
            p.sy.wait(p.va)
            p.sy.dma_start(
                out=xout.rearrange("(ct p) t -> p ct t", p=128)
                    [:, :, tc * 512:(tc + 1) * 512],
                in_=xst[:, 0:8 * 512].rearrange("p (ct t) -> p ct t", ct=8)).inc(s_o, 16)
        p.sy.wait(s_o)
        if dbg:
            p.sy.dma_start(out=d_qn.rearrange("(a p) t -> p a t", p=128),
                           in_=qn.ap().rearrange("p (a t) -> p a t", a=8)).inc(s_o, 16)
            p.sy.dma_start(out=d_kn.rearrange("(a p) t -> p a t", p=128),
                           in_=kn.ap().rearrange("p (a t) -> p a t", a=8)).inc(s_o, 16)
            p.sy.dma_start(out=d_vt.rearrange("(a p) t -> p a t", p=128),
                           in_=vtm.ap().rearrange("p (a t) -> p a t", a=20)).inc(s_o, 16)
            p.sy.dma_start(out=d_oa.rearrange("(a p) t -> p a t", p=128),
                           in_=oaT.ap().rearrange("p (a t) -> p a t", a=8)).inc(s_o, 16)
            p.sy.wait(s_o)
        for cm in [wos_cm, rdc_cm, rsc_cm, mxc_cm, oaT_cm]:
            cm.__exit__(None, None, None)
        p.emit()
    return nc



def build_l1moe(nc):
    xres = decl_in(nc, "xres", (D, T))
    xout = decl_out(nc, "xout", (D, T))
    wr = decl_in(nc, "l1_wr", (128, 8, 1, 8))
    w01 = decl_in(nc, "l1_w01t", (8, 43, 128, 2, 8, 128))
    wot = decl_in(nc, "l1_wot", (8, 43, 128, 2, 512))
    ones = decl_in(nc, "ones_col", (128, 1))
    onesr = decl_in(nc, "ones_row", (1, 128))
    ident = decl_in(nc, "ident", (128, 128))
    iota8 = decl_in(nc, "iota8", (128, 8))
    tri = decl_in(nc, "tri_excl", (128, 128))
    iota512_in = decl_in(nc, "iota512", (128, 512))
    CAP = 512.0

    with ExitStack() as st:
        E = st.enter_context
        p = Prog(nc, st)
        onec = E(nc.sbuf_tensor("onec", [128, 1], F32))
        onerow = E(nc.sbuf_tensor("onerow", [1, 128], F32))
        idc = E(nc.sbuf_tensor("idc", [128, 128], BF16))
        io8 = E(nc.sbuf_tensor("io8", [128, 8], F32))
        tri_sb = E(nc.sbuf_tensor("tri_sb", [128, 128], F32))
        io512 = E(nc.sbuf_tensor("io512", [128, 512], F32))
        wrs = E(nc.sbuf_tensor("wrs", [128, 64], BF16))
        lgt = E(nc.sbuf_tensor("lgt", [128, 8], F32))
        gsc = E(nc.sbuf_tensor("gsc", [128, 64], F32))
        m1s = E(nc.sbuf_tensor("m1s", [128, 128], F32))
        m2s = E(nc.sbuf_tensor("m2s", [128, 128], F32))
        m1k = E(nc.sbuf_tensor("m1k", [128, 128], F32))
        cols = E(nc.sbuf_tensor("cols", [128, 12 * 16], F32))
        carry = E(nc.sbuf_tensor("carry", [1, 16], F32))
        qs12 = E(nc.sbuf_tensor("qs12", [128, 32], F32))
        psA = E(nc.psum_tensor("psA", [128, 1024], F32))
        psB = E(nc.psum_tensor("psB", [128, 1024], F32))
        psC = E(nc.psum_tensor("psC", [128, 512], F32))
        psD = E(nc.psum_tensor("psD", [128, 512], F32))
        psT = E(nc.psum_tensor("psT", [128, 1024], BF16))
        C = dict(psC=psC, psD=psD, psB=psB, onec=onec, onerow=onerow)
        s_c = p.sem("s_c"); s_t = p.sem("s_t")
        s_o = p.sem("s_o"); s_x = p.sem("s_x")

        for ap_, dst in [(ones, onec), (onesr, onerow), (iota8, io8),
                         (tri, tri_sb), (iota512_in, io512)]:
            p.gp.dma_start(out=dst[:], in_=ap_).inc(s_c, 16)
        p.gp.dma_start(out=idc[:], in_=ident).inc(s_c, 16)
        p.gp.dma_start(out=wrs[:], in_=wr.rearrange("p a b c -> p (a b c)")).inc(s_c, 16)
        p.te.wait(s_c); p.ve.wait(s_c); p.sc.wait(s_c)

        eiT_cm = nc.sbuf_tensor("eiT", [128, 8 * 4096], BF16); eiT = eiT_cm.__enter__()
        xtm_cm = nc.sbuf_tensor("xtm", [128, 16 * 1024], BF16); xtm = xtm_cm.__enter__()
        xfm_cm = nc.sbuf_tensor("xfm", [128, 8 * T], BF16); xfm = xfm_cm.__enter__()

        ln = emit_ln(p, C, xres, xfm, tag="lnm")
        p.te.wait(p.va, ln["v_done"])

        for tt in range(16):
            for ct in range(8):
                if tt + ct > 0:
                    p.te.wait(p.va)
                p.te.transpose(psT[:, 0:128],
                               xfm[:, ct * T + tt * 128: ct * T + (tt + 1) * 128],
                               idc[:]).inc(s_t)
                p.ve.wait(s_t)
                p.ve.tensor_copy(xtm[:, tt * 1024 + ct * 128: tt * 1024 + (ct + 1) * 128],
                                 psT[:, 0:128])

        for tt in range(16):
            p.te.wait(p.va)
            for ct in range(8):
                tok = p.te.matmul(psC[:, 0:8],
                                  xfm[:, ct * T + tt * 128: ct * T + (tt + 1) * 128],
                                  wrs[:, ct * 8:(ct + 1) * 8],
                                  start=(ct == 0), stop=(ct == 7))
                if ct == 7:
                    tok.inc(s_t)
            p.ve.wait(s_t)
            p.ve.tensor_copy(lgt[:], psC[:, 0:8])
            mxg = gsc[:, 56:57]; raw = gsc[:, 40:48]; smg = gsc[:, 62:63]
            p.ve.reduce_max(mxg, lgt[:], axis=AX.X, negate=True)
            p.sc.wait(p.va)
            p.sc.activation(out=raw, in_=lgt[:], func=AF.Exp, bias=mxg[:, 0:1],
                            accum_out=smg)
            p.ve.wait(p.sa)
            p.ve.reciprocal(gsc[:, 63:64], smg)
            p.ve.wait(p.va)
            p.ve.tensor_scalar(out=raw, in0=raw, scalar1=gsc[:, 63:64],
                               scalar2=None, op0=ALU.mult)
            m1 = m1s[:, tt * 8:(tt + 1) * 8]
            m2 = m2s[:, tt * 8:(tt + 1) * 8]
            g1 = cols[:, tt * 12 + 0: tt * 12 + 1]
            g2 = cols[:, tt * 12 + 1: tt * 12 + 2]
            i1 = cols[:, tt * 12 + 2: tt * 12 + 3]
            i2 = cols[:, tt * 12 + 3: tt * 12 + 4]
            first_argmax(p, m1, raw, g1, gsc[:, 16:40])
            w8 = gsc[:, 48:56]
            p.ve.tensor_scalar(out=w8, in0=m1, scalar1=-1.0, scalar2=1.0,
                               op0=ALU.mult, op1=ALU.add)
            p.ve.tensor_mul(w8, w8, raw)
            first_argmax(p, m2, w8, g2, gsc[:, 16:40])
            t8 = gsc[:, 0:8]
            p.ve.tensor_mul(t8, m1, io8[:])
            p.ve.reduce_sum(i1, t8, axis=AX.X)
            p.ve.tensor_mul(t8, m2, io8[:])
            p.ve.reduce_sum(i2, t8, axis=AX.X)

        def cap_pass(msrc, gidx, iidx, qidx, gkidx, init_from_saved):
            p.ve.memset(carry[0:1, 0:8], 0.0)
            if init_from_saved:
                p.ve.tensor_copy(carry[0:1, 0:8], carry[0:1, 8:16])
            for tt in range(16):
                mt_ = msrc[:, tt * 8:(tt + 1) * 8]
                p.te.wait(p.va)
                p.te.matmul(psC[:, 0:8], tri_sb[:], mt_, start=True, stop=False)
                p.te.matmul(psC[:, 0:8], onerow[:], carry[0:1, 0:8],
                            start=False, stop=True).inc(s_t)
                p.te.matmul(psA[:1, 0:8], onec[:], mt_, start=True, stop=True).inc(s_t)
                p.ve.wait(s_t)
                p1t = gsc[:, 0:8]
                p.ve.tensor_copy(p1t, psC[:, 0:8])
                p.ve.tensor_add(carry[0:1, 0:8], carry[0:1, 0:8], psA[:1, 0:8])
                lt = gsc[:, 8:16]
                p.ve.tensor_scalar(out=lt, in0=p1t, scalar1=CAP, scalar2=None,
                                   op0=ALU.is_lt)
                mk = m1k[:, tt * 8:(tt + 1) * 8]
                p.ve.tensor_mul(mk, mt_, lt)
                mkm = p.va.n
                p.te.wait(p.va, mkm)
                p.te.matmul(psD[:1, 0:8], onec[:], mk,
                            start=(tt == 0), stop=(tt == 15)).inc(s_t)
                pk = gsc[:, 16:24]
                p.ve.tensor_mul(pk, p1t, mk)
                mf = gsc[:, 57:58]
                p.ve.reduce_sum(mf, mk, axis=AX.X)
                pkc = gsc[:, 58:59]
                p.ve.reduce_sum(pkc, pk, axis=AX.X)
                q = gsc[:, 59:60]
                p.ve.wait(p.va)
                p.ve.scalar_tensor_tensor(out=q,
                                          in0=cols[:, tt * 12 + iidx: tt * 12 + iidx + 1],
                                          scalar=CAP, in1=pkc, op0=ALU.mult, op1=ALU.add)
                wv_ = gsc[:, 60:61]
                p.ve.tensor_scalar(out=wv_, in0=mf, scalar1=-50000.0, scalar2=50000.0,
                                   op0=ALU.mult, op1=ALU.add)
                qm = gsc[:, 61:62]
                p.ve.tensor_mul(qm, q, mf)
                p.ve.tensor_add(cols[:, tt * 12 + qidx: tt * 12 + qidx + 1], qm, wv_)
                p.ve.tensor_mul(cols[:, tt * 12 + gkidx: tt * 12 + gkidx + 1],
                                cols[:, tt * 12 + gidx: tt * 12 + gidx + 1], mf)
            p.ve.wait(s_t)
            p.ve.tensor_copy(carry[0:1, 8:16], psD[:1, 0:8])

        cap_pass(m1s, 0, 2, 4, 6, False)
        cap_pass(m2s, 1, 3, 5, 7, True)

        fence(p, s_t)
        xfm_cm.__exit__(None, None, None)
        dsb_cm = nc.sbuf_tensor("dsb", [128, 16 * 512], BF16); dsb = dsb_cm.__enter__()

        qv = cols.ap().rearrange("p (t c) -> p t c", c=12)
        for ecc in range(8):
            p.ve.wait(s_t)
            p.ve.tensor_scalar(out=qs12[:, 0:16], in0=qv[:, :, 4],
                               scalar1=float(-512.0 * ecc), scalar2=None, op0=ALU.add)
            p.ve.tensor_scalar(out=qs12[:, 16:32], in0=qv[:, :, 5],
                               scalar1=float(-512.0 * ecc), scalar2=None, op0=ALU.add)
            qmk = p.va.n
            for tt in range(16):
                p.ve.wait(p.va, qmk)
                dt_ = dsb[:, tt * 512:(tt + 1) * 512]
                p.ve.tensor_scalar(out=dt_, in0=io512[:], scalar1=qs12[:, tt:tt + 1],
                                   scalar2=None, op0=ALU.is_equal)
                p.ve.scalar_tensor_tensor(out=dt_, in0=io512[:],
                                          scalar=qs12[:, 16 + tt:17 + tt], in1=dt_,
                                          op0=ALU.is_equal, op1=ALU.add)
            dmk = p.va.n
            p.te.wait(p.va, dmk)
            for mt in range(8):
                ps = (psC if mt % 2 == 0 else psD)[:, 0:512]
                if ecc * 8 + mt >= 2:
                    p.te.wait(p.va)
                for tt in range(16):
                    tok = p.te.matmul(ps,
                                      xtm[:, tt * 1024 + mt * 128: tt * 1024 + (mt + 1) * 128],
                                      dsb[:, tt * 512:(tt + 1) * 512],
                                      start=(tt == 0), stop=(tt == 15))
                    if tt == 15:
                        tok.inc(s_t)
                p.ve.wait(s_t)
                p.ve.tensor_copy(eiT[:, mt * 4096 + ecc * 512: mt * 4096 + (ecc + 1) * 512],
                                 ps)

        fence(p, s_t)
        dsb_cm.__exit__(None, None, None)
        xtm_cm.__exit__(None, None, None)
        eo_cm = nc.sbuf_tensor("eo", [128, 32 * 1024], BF16); eo = eo_cm.__enter__()
        hid_cm = nc.sbuf_tensor("hid", [128, 43 * 512], BF16); hid = hid_cm.__enter__()
        rg1_cm = nc.sbuf_tensor("rg1", [128, 4 * 2048], BF16); rg1 = rg1_cm.__enter__()
        rg2_cm = nc.sbuf_tensor("rg2", [128, 4 * 512], BF16); rg2 = rg2_cm.__enter__()
        gl_cm = nc.sbuf_tensor("gl", [128, 512], BF16); gl = gl_cm.__enter__()
        s_w1 = p.sem("s_w1"); s_w2 = p.sem("s_w2")
        w1cons = {}
        w2cons = {}
        for e in range(8):
            for ht in range(43):
                k = e * 43 + ht
                slot = k % 4
                if k >= 4:
                    p.gp.wait(s_t, w1cons.get(k - 4, 0))
                p.gp.dma_start(out=rg1[:, slot * 2048:(slot + 1) * 2048],
                               in_=w01[e, ht].rearrange("p a b c -> p (a b c)")).inc(s_w1, 16)
                p.te.wait(s_w1, (k + 1) * 16)
                pp = ht % 2
                h0 = psA[:, pp * 512:(pp + 1) * 512]
                h1 = psB[:, pp * 512:(pp + 1) * 512]
                if ht >= 2:
                    p.te.wait(p.sa)
                    p.te.wait(p.va)
                for vg in range(2):
                    hp = h0 if vg == 0 else h1
                    for mt in range(8):
                        tok = p.te.matmul(hp,
                                          rg1[:, slot * 2048 + (vg * 8 + mt) * 128:
                                              slot * 2048 + (vg * 8 + mt + 1) * 128],
                                          eiT[:, mt * 4096 + e * 512: mt * 4096 + (e + 1) * 512],
                                          start=(mt == 0), stop=(mt == 7))
                        if vg == 1 and mt == 7:
                            tok.inc(s_t)
                w1cons[k] = s_t.n
                p.sc.wait(s_t, w1cons[k])
                p.sc.wait(p.va)
                p.sc.activation(out=gl[:], in_=h0, func=AF.Gelu_apprx_tanh)
                p.ve.wait(p.sa)
                p.ve.tensor_tensor(out=hid[:, ht * 512:(ht + 1) * 512], in0=h1,
                                   in1=gl[:], op=ALU.mult)
            hidm = p.va.n
            p.te.wait(p.va, hidm)
            for mc in range(2):
                for ht in range(43):
                    k2 = (e * 2 + mc) * 43 + ht
                    slot = k2 % 4
                    if k2 >= 4:
                        p.gp.wait(s_t, w2cons.get(k2 - 4, 0))
                    p.gp.dma_start(out=rg2[:, slot * 512:(slot + 1) * 512],
                                   in_=wot[e, ht, :, mc]).inc(s_w2, 16)
                    p.te.wait(s_w2, (k2 + 1) * 16)
                    for sc_ in range(4):
                        ps = (psA if sc_ < 2 else psB)[:, (sc_ % 2) * 512:(sc_ % 2 + 1) * 512]
                        tok = p.te.matmul(ps,
                                          hid[:, ht * 512 + sc_ * 128: ht * 512 + (sc_ + 1) * 128],
                                          rg2[:, slot * 512:(slot + 1) * 512],
                                          start=(ht == 0), stop=(ht == 42))
                        if sc_ == 3:
                            tok.inc(s_t)
                    w2cons[k2] = s_t.n
                p.ve.wait(s_t)
                for sc_ in range(4):
                    ps = (psA if sc_ < 2 else psB)[:, (sc_ % 2) * 512:(sc_ % 2 + 1) * 512]
                    p.ve.tensor_copy(
                        eo[:, (e * 4 + sc_) * 1024 + mc * 512:
                           (e * 4 + sc_) * 1024 + (mc + 1) * 512], ps)
                p.te.wait(p.va)

        fence(p, s_t)
        for cm in [gl_cm, rg2_cm, rg1_cm, hid_cm]:
            cm.__exit__(None, None, None)
        cT_cm = nc.sbuf_tensor("cT", [128, 32 * 512], BF16); cT = cT_cm.__enter__()
        ctm_cm = nc.sbuf_tensor("ctm", [128, 2 * 512], BF16); ctm = ctm_cm.__enter__()
        xst_cm = nc.sbuf_tensor("xst", [128, 8 * 512], F32); xst = xst_cm.__enter__()

        for tc in range(4):
            for ecc in range(8):
                p.ve.wait(s_t)
                p.ve.tensor_scalar(out=qs12[:, 0:16], in0=qv[:, :, 4],
                                   scalar1=float(-512.0 * ecc), scalar2=None, op0=ALU.add)
                p.ve.tensor_scalar(out=qs12[:, 16:32], in0=qv[:, :, 5],
                                   scalar1=float(-512.0 * ecc), scalar2=None, op0=ALU.add)
                for ttl in range(4):
                    tt = tc * 4 + ttl
                    p.ve.tensor_scalar(out=ctm[:, 0:512], in0=io512[:],
                                       scalar1=qs12[:, tt:tt + 1],
                                       scalar2=cols[:, tt * 12 + 6: tt * 12 + 7],
                                       op0=ALU.is_equal, op1=ALU.mult)
                    p.ve.tensor_scalar(out=ctm[:, 512:1024], in0=io512[:],
                                       scalar1=qs12[:, 16 + tt:17 + tt],
                                       scalar2=cols[:, tt * 12 + 7: tt * 12 + 8],
                                       op0=ALU.is_equal, op1=ALU.mult)
                    p.ve.tensor_add(ctm[:, 0:512], ctm[:, 0:512], ctm[:, 512:1024])
                    cmk = p.va.n
                    p.te.wait(p.va, cmk)
                    for sub in range(4):
                        et = ecc * 4 + sub
                        tok = p.te.transpose(psT[:, sub * 128:(sub + 1) * 128],
                                             ctm[:, 0:512][:, sub * 128:(sub + 1) * 128],
                                             idc[:])
                        if sub == 3:
                            tok.inc(s_t)
                    p.ve.wait(s_t)
                    for sub in range(4):
                        et = ecc * 4 + sub
                        p.ve.tensor_copy(cT[:, et * 512 + ttl * 128: et * 512 + (ttl + 1) * 128],
                                         psT[:, sub * 128:(sub + 1) * 128])
            if tc >= 1:
                p.sy.wait(s_o, tc * 16)
            p.sy.wait(p.va)
            p.sy.dma_start(
                out=xst[:, 0:8 * 512].rearrange("p (ct t) -> p ct t", ct=8),
                in_=xres.rearrange("(ct p) t -> p ct t", p=128)
                    [:, :, tc * 512:(tc + 1) * 512]).inc(s_x, 16)
            p.ve.wait(s_x)
            cTm = p.va.n
            p.te.wait(p.va, cTm)
            for mt in range(8):
                ps = (psC if mt % 2 == 0 else psD)[:, 0:512]
                if mt >= 2 or tc > 0:
                    p.te.wait(p.va)
                for et in range(32):
                    tok = p.te.matmul(ps, eo[:, et * 1024 + mt * 128: et * 1024 + (mt + 1) * 128],
                                      cT[:, et * 512:(et + 1) * 512],
                                      start=(et == 0), stop=(et == 31))
                    if et == 31:
                        tok.inc(s_t)
                p.ve.wait(s_t)
                p.ve.scalar_tensor_tensor(out=xst[:, mt * 512:(mt + 1) * 512],
                                          in0=ps, scalar=0.5,
                                          in1=xst[:, mt * 512:(mt + 1) * 512],
                                          op0=ALU.mult, op1=ALU.add)
            p.sy.wait(p.va)
            p.sy.dma_start(
                out=xout.rearrange("(ct p) t -> p ct t", p=128)
                    [:, :, tc * 512:(tc + 1) * 512],
                in_=xst[:, 0:8 * 512].rearrange("p (ct t) -> p ct t", ct=8)).inc(s_o, 16)
        p.sy.wait(s_o)
        for cm in [xst_cm, ctm_cm, cT_cm, eo_cm, eiT_cm]:
            cm.__exit__(None, None, None)
        p.emit()
    return nc



_L0_KEYS = ["l0_w1t", "l0_w2t", "l0_wr", "ones_col", "ones_row", "ident", "iota8"]
_MBPRE_KEYS = ["mb_inprojT", "mb_inproj_b", "ones_col", "ones_row"]
_MBCONV_KEYS = ["mb_convw_col", "mb_convb_col", "mb_bcT", "mb_bc_b", "mb_dtT",
                "mb_dt_b", "ones_col", "ones_row"]
_MBPOST_KEYS = ["mb_outT", "mb_out_b", "mb_D_col", "mb_lng", "mb_lnb",
                "ones_col", "ones_row"]
_ATTN_KEYS = ["at_wqT", "at_wkT", "at_wvT", "at_woT", "at_qnw", "at_knw",
              "ones_col", "ones_row", "ident", "mask0", "alibi_flat"]
_L1_KEYS = ["l1_wr", "l1_w01t", "l1_wot", "ones_col", "ones_row", "ident",
            "iota8", "tri_excl", "iota512"]

_NC_CACHE = {}


def _get_nc(name, builder):
    if name not in _NC_CACHE:
        _NC_CACHE[name] = make_nc(builder)
    return _NC_CACHE[name]


def _run(name, builder, in_maps):
    nc = _get_nc(name, builder)
    return run_bass_kernel_spmd(nc, in_maps, core_ids=list(range(NCORE))).results


def _run_l0(P, xc):
    res = _run("l0", build_l0ffn, [dict(xres=xc[c], **{k: P[k] for k in _L0_KEYS})
                                   for c in range(NCORE)])
    return [np.asarray(res[c]["xout"]) for c in range(NCORE)]


def _run_l1(P, xc):
    res = _run("l1", build_l1moe, [dict(xres=xc[c], **{k: P[k] for k in _L1_KEYS})
                                   for c in range(NCORE)])
    return [np.asarray(res[c]["xout"]) for c in range(NCORE)]


def _run_attn(P, xc):
    in_maps = []
    for c in range(NCORE):
        ext = np.zeros((D, T + 512), np.float32)
        if c % 2 == 1:
            ext[:, :512] = xc[c - 1][:, -512:]
        ext[:, 512:] = xc[c]
        m = {k: P[k] for k in _ATTN_KEYS}
        m["edge_mask"] = P["edge_mask_even"] if c % 2 == 0 else P["edge_mask_odd"]
        m["xres_ext"] = ext
        in_maps.append(m)
    res = _run("attn", build_attn, in_maps)
    return [np.asarray(res[c]["xout"]) for c in range(NCORE)]


def _run_mamba(P, xc):
    res = _run("mbpre", build_mb_pre,
               [dict(xres=xc[c], **{k: P[k] for k in _MBPRE_KEYS})
                for c in range(NCORE)])
    u_c = [np.asarray(res[c]["u"]) for c in range(NCORE)]
    xssm = [np.asarray(res[c]["xssm"]).view(BF) for c in range(NCORE)]
    xgate = [np.asarray(res[c]["xgate"]).view(BF) for c in range(NCORE)]
    in_maps = []
    for c in range(NCORE):
        halo = np.zeros((D, 4), BF)
        if c % 2 == 1:
            halo[:, 1:4] = xssm[c - 1][:, -3:]
        in_maps.append(dict(xssm=xssm[c], halo3=halo,
                            **{k: P[k] for k in _MBCONV_KEYS}))
    res = _run("mbconv", build_mb_conv, in_maps)
    xconv = [np.asarray(res[c]["xconv"]).view(BF) for c in range(NCORE)]
    bc_c = [np.asarray(res[c]["bc"]) for c in range(NCORE)]
    dt_c = [np.asarray(res[c]["dtT"]) for c in range(NCORE)]
    in_maps = []
    for c in range(NCORE):
        b, hh = c // 2, c % 2
        xch = np.concatenate([xconv[2 * b], xconv[2 * b + 1]], axis=1)[hh * 512:(hh + 1) * 512]
        dth = np.concatenate([dt_c[2 * b], dt_c[2 * b + 1]], axis=1)[hh * 512:(hh + 1) * 512]
        bcf = np.concatenate([bc_c[2 * b], bc_c[2 * b + 1]], axis=0)
        in_maps.append(dict(xch=np.ascontiguousarray(xch),
                            dth=np.ascontiguousarray(dth.astype(np.float32)),
                            bf=np.ascontiguousarray(bcf[:, :16].astype(np.float32).T),
                            cf=np.ascontiguousarray(bcf[:, 16:].astype(np.float32).T),
                            alogh=np.ascontiguousarray(
                                P["mb_Alog_col"][:, hh * 64:(hh + 1) * 64]),
                            ones_row=P["ones_row"]))
    res = _run("mbscan", build_mb_scan, in_maps)
    yh = [np.asarray(res[c]["yh"]).view(BF) for c in range(NCORE)]
    in_maps = []
    for c in range(NCORE):
        b, hh = c // 2, c % 2
        ybh = np.concatenate([yh[2 * b], yh[2 * b + 1]], axis=0)
        ymine = np.ascontiguousarray(ybh[:, hh * T:(hh + 1) * T])
        in_maps.append(dict(y=ymine, xconv=xconv[c], xgate=xgate[c],
                            u=u_c[c], xres=xc[c],
                            **{k: P[k] for k in _MBPOST_KEYS}))
    res = _run("mbpost", build_mb_post, in_maps)
    return [np.asarray(res[c]["xout"]) for c in range(NCORE)]


def kernel(**inputs):
    P = prep_weights(inputs)
    x = _np(inputs["L0_attn_x"])
    xc = [np.ascontiguousarray(x[c // 2, (c % 2) * T:(c % 2 + 1) * T, :].T)
          for c in range(NCORE)]
    xc = _run_l0(P, xc)
    xc = _run_mamba(P, xc)
    xc = _run_l0(P, xc)
    xc = _run_l1(P, xc)
    xc = _run_attn(P, xc)
    xc = _run_l1(P, xc)
    out = np.zeros((4, 4096, 1024), np.float32)
    for c in range(NCORE):
        out[c // 2, (c % 2) * T:(c % 2 + 1) * T, :] = xc[c].T
    return out


if __name__ == "__main__":
    pass
